# version 04 (final)
# baseline (speedup 1.0000x reference)
import numpy as np
import concourse.bass as bass
import concourse.mybir as mybir
from concourse.bass_utils import run_bass_kernel_spmd
from contextlib import ExitStack

F32 = mybir.dt.float32
BF16 = mybir.dt.bfloat16
AF = mybir.ActivationFunctionType
ALU = mybir.AluOpType
AX = mybir.AxisListType

D = 2048
T = 1024
TW = 1025
NJ = 16
E = 8192
NE = 64
G8 = [[0, 1, 2, 3, 4, 5, 6, 7]]
NELEM = float(E * D)
CH = [(0, 512), (512, 1024)]
CH1 = [(0, 512), (512, 1024), (1024, 1025)]


class Sems:
    def __init__(self, ctx, nc):
        self.h = {}
        self.n = {}
        for nm in ["ld", "gp", "cc", "tn", "vt", "sc"]:
            self.h[nm] = ctx.enter_context(nc.semaphore(nm))
            self.n[nm] = 0

    def inc(self, nm, instr, k=1):
        instr.then_inc(self.h[nm], k)
        self.n[nm] += k
        return self.n[nm]

    def w(self, eng, nm, v):
        eng.wait_ge(self.h[nm], v)


class Banks:
    def __init__(self, pbs):
        self.pbs = pbs
        self.pend = [[] for _ in pbs]

    def use(self, eng, S, i):
        for nm, v in self.pend[i]:
            eng.wait_ge(S.h[nm], v)
        self.pend[i] = []
        return self.pbs[i]

    def free_when(self, i, nm, v):
        self.pend[i].append((nm, v))


class Ring:
    def __init__(self, slots):
        self.slots = slots
        self.pend = [[] for _ in slots]
        self.i = 0

    def nxt(self, eng, S):
        i = self.i % len(self.slots)
        self.i += 1
        for nm, v in self.pend[i]:
            eng.wait_ge(S.h[nm], v)
        self.pend[i] = []
        return i, self.slots[i]

    def free_when(self, i, nm, v):
        self.pend[i].append((nm, v))


def build():
    nc = bass.Bass("TRN2", target_bir_lowering=False)

    XT = nc.dram_tensor("XT", [D, TW], F32, kind="ExternalInput").ap()
    WKT = nc.dram_tensor("WKT", [D, D], F32, kind="ExternalInput").ap()
    WVT = nc.dram_tensor("WVT", [D, D], F32, kind="ExternalInput").ap()
    WRT = nc.dram_tensor("WRT", [D, D], F32, kind="ExternalInput").ap()
    WUT = nc.dram_tensor("WUT", [D, E], F32, kind="ExternalInput").ap()
    WGT = nc.dram_tensor("WGT", [D, E], F32, kind="ExternalInput").ap()
    WDT = nc.dram_tensor("WDT", [E, D], F32, kind="ExternalInput").ap()
    WMODS = nc.dram_tensor("WMODS", [256, 6 * D], F32, kind="ExternalInput").ap()
    CTS = nc.dram_tensor("CTS", [128, 2, 2], F32, kind="ExternalInput").ap()
    TDI = nc.dram_tensor("TDI", [128, NJ], F32, kind="ExternalInput").ap()
    CONSTS = nc.dram_tensor("CONSTS", [128, 20], F32, kind="ExternalInput").ap()
    SUU = nc.dram_tensor("SUU", [D, 1024], F32, kind="ExternalInput").ap()
    SUG = nc.dram_tensor("SUG", [D, 1024], F32, kind="ExternalInput").ap()
    SUD = nc.dram_tensor("SUD", [E, 256], F32, kind="ExternalInput").ap()
    OUT = nc.dram_tensor("OUT", [128, NJ, T], F32, kind="ExternalOutput").ap()

    mod_in = nc.dram_tensor("mod_in", [128, 192], F32)
    mod_out = nc.dram_tensor("mod_out", [128, 192], F32)
    su_in = nc.dram_tensor("su_in", [1, 4], F32)
    su_out = nc.dram_tensor("su_out", [1, 4], F32)
    fin_in = nc.dram_tensor("fin_in", [128, 32], F32)
    fin_out = nc.dram_tensor("fin_out", [1024, 32], F32)
    park = nc.dram_tensor("park", [128, NJ, T], F32)
    hidspill = nc.dram_tensor("hidspill", [128, NE, T], mybir.dt.bfloat16)

    M = {}
    with ExitStack() as P:
        eP = P.enter_context
        S = Sems(P, nc)
        pbs = [eP(nc.psum_tensor(f"pb{i}", [128, 512], F32)) for i in range(8)]
        BK = Banks(pbs)
        blk = eP(nc.Block())

        cst = eP(nc.sbuf_tensor("cst", [128, 20], F32))
        td = eP(nc.sbuf_tensor("td", [128, NJ], F32))
        expt = eP(nc.sbuf_tensor("expt", [128, NJ], F32))
        logw = eP(nc.sbuf_tensor("logw", [128, NJ], F32))
        wvec = eP(nc.sbuf_tensor("wvec", [128, NJ], F32))
        modT = eP(nc.sbuf_tensor("modT", [128, 96, 2], F32))
        modv = eP(nc.sbuf_tensor("modv", [128, 96], F32))
        ca1 = eP(nc.sbuf_tensor("ca1", [128, NJ], F32))
        cb1 = eP(nc.sbuf_tensor("cb1", [128, NJ], F32))
        ga_f = eP(nc.sbuf_tensor("ga_f", [128, NJ], F32))
        gb_f = eP(nc.sbuf_tensor("gb_f", [128, NJ], F32))
        gbsd = eP(nc.sbuf_tensor("gbsd", [128, NJ], F32))
        sgn16 = eP(nc.sbuf_tensor("sgn16", [128, NJ], F32))
        ab16 = eP(nc.sbuf_tensor("ab16", [128, NJ], F32))
        coefful = eP(nc.sbuf_tensor("coefful", [128, 8, NJ, 2], F32))
        fins = eP(nc.sbuf_tensor("fins", [128, 32], F32))
        fing = eP(nc.sbuf_tensor("fing", [128, 8, 32], F32))
        prodb = eP(nc.sbuf_tensor("prodb", [128, 8, 32], F32))
        red4 = eP(nc.sbuf_tensor("red4", [128, 4, 32], F32))
        red2 = eP(nc.sbuf_tensor("red2", [128, 2, 32], F32))
        carry = eP(nc.sbuf_tensor("carry", [128, 32], F32))
        ctsilu = eP(nc.sbuf_tensor("ctsilu", [128, 2, 2], BF16))
        cts_sb = eP(nc.sbuf_tensor("cts_sb", [128, 2, 2], F32))
        ones_bf = eP(nc.sbuf_tensor("ones_bf", [128, 1], BF16))
        ones_row = eP(nc.sbuf_tensor("ones_row", [1, 128], F32))
        one_col = eP(nc.sbuf_tensor("one_col", [128, 1], F32))
        epsln = eP(nc.sbuf_tensor("epsln", [128, 1], F32))
        suacc = eP(nc.sbuf_tensor("suacc", [128, 24], F32))
        su_sb = eP(nc.sbuf_tensor("su_sb", [1, 4], F32))
        su1 = eP(nc.sbuf_tensor("su1", [1, 4], F32))
        su_bc = eP(nc.sbuf_tensor("su_bc", [128, 3], F32))
        su2 = eP(nc.sbuf_tensor("su2", [128, 3], F32))
        nsu2 = eP(nc.sbuf_tensor("nsu2", [128, 3], F32))
        stf = [eP(nc.sbuf_tensor(f"stf{i}", [128, NJ, 128], F32)) for i in range(2)]
        stF = Ring(stf)
        wbfA = eP(nc.sbuf_tensor("wbfA", [128, NJ, 128], BF16))
        wbfB = eP(nc.sbuf_tensor("wbfB", [128, NJ, 128], BF16))
        h2 = eP(nc.sbuf_tensor("h2", [128, NJ, T], BF16))

        with ExitStack() as AB:
            eAB = AB.enter_context
            h_bf = eAB(nc.sbuf_tensor("h_bf", [128, NJ, TW], BF16))
            s_mean = eAB(nc.sbuf_tensor("s_mean", [1, TW], F32))
            s_tmp = eAB(nc.sbuf_tensor("s_tmp", [1, TW], F32))
            rs_bc = eAB(nc.sbuf_tensor("rs_bc", [128, TW], F32))
            mrs_bc = eAB(nc.sbuf_tensor("mrs_bc", [128, TW], F32))
            xbf = eAB(nc.sbuf_tensor("xbf", [128, TW], BF16))
            xsq = eAB(nc.sbuf_tensor("xsq", [128, TW], BF16))
            rsrow = eAB(nc.sbuf_tensor("rsrow", [1, TW], F32))
            mrsrow = eAB(nc.sbuf_tensor("mrsrow", [1, TW], F32))

            with ExitStack() as A:
                eA = A.enter_context
                x_sb = eA(nc.sbuf_tensor("x_sb", [128, NJ, TW], F32))
                tmpW = eA(nc.sbuf_tensor("tmpW", [128, TW], F32))
                wmf = [eA(nc.sbuf_tensor(f"wmf{i}", [128, 2, 256], F32)) for i in range(2)]
                wmb = [eA(nc.sbuf_tensor(f"wmb{i}", [128, 2, 256], BF16)) for i in range(2)]

                @blk.sync
                def _(eng):
                    S.inc("ld", eng.dma_start(out=cst[:], in_=CONSTS), 16)
                    S.inc("ld", eng.dma_start(out=td[:], in_=TDI), 16)
                    S.inc("ld", eng.dma_start(out=cts_sb[:], in_=CTS), 16)
                M["ld_small"] = S.n["ld"]

                @blk.sync
                def _(eng):
                    for g in range(4):
                        S.inc("ld", eng.dma_start(
                            out=x_sb[:, 4 * g:4 * g + 4, :],
                            in_=XT.rearrange("(j p) t -> p j t", p=128)[:, 4 * g:4 * g + 4, :]), 16)
                M["ld_xg"] = [M["ld_small"] + 16 * (g + 1) for g in range(4)]

                @blk.scalar
                def _(eng):
                    S.w(eng, "ld", M["ld_small"])
                    S.inc("sc", eng.activation(ctsilu[:], cts_sb[:], AF.Silu))
                    M["sc_silu"] = S.n["sc"]
                    S.inc("sc", eng.activation(expt[:], td[:], AF.Exp))
                    M["sc_exptd"] = S.n["sc"]

                @blk.vector
                def _(eng):
                    eng.memset(ones_bf[:], 1.0)
                    eng.memset(ones_row[:], 1.0)
                    eng.memset(one_col[:], 1.0)
                    eng.memset(epsln[:], 1e-5)
                    eng.memset(su_sb[:], 0.0)
                    S.w(eng, "sc", M["sc_exptd"])
                    S.inc("vt", eng.tensor_scalar(logw[:], expt[:], -1.0, 0.0, ALU.mult, ALU.add))
                    M["vt_logw"] = S.n["vt"]
                    S.w(eng, "vt", M["vt_logw"])
                    for j in range(8):
                        S.inc("vt", eng.tensor_scalar(coefful[:, j, :, 0], logw[:],
                                                      cst[:, 3 + j:4 + j], 0.0, ALU.mult, ALU.add))
                    M["vt_coefarg"] = S.n["vt"]

                @blk.scalar
                def _(eng):
                    S.w(eng, "vt", M["vt_logw"])
                    S.inc("sc", eng.activation(wvec[:], logw[:], AF.Exp))
                    M["sc_wv"] = S.n["sc"]
                    S.w(eng, "vt", M["vt_coefarg"])
                    for j in range(8):
                        S.inc("sc", eng.activation(coefful[:, j, :, 0], coefful[:, j, :, 0], AF.Exp))
                    M["sc_coefexp"] = S.n["sc"]

                @blk.vector
                def _(eng):
                    S.w(eng, "sc", M["sc_coefexp"])
                    for j in range(8):
                        S.inc("vt", eng.tensor_scalar(coefful[:, j, :, 0], coefful[:, j, :, 0],
                                                      cst[:, 11 + j:12 + j], 0.0, ALU.mult, ALU.add))
                        S.w(eng, "vt", S.n["vt"])
                        S.inc("vt", eng.tensor_copy(out=coefful[:, j, :, 1], in_=coefful[:, j, :, 0]))
                    M["vt_coef"] = S.n["vt"]

                for ci in range(48):
                    @blk.sync
                    def _(eng, ci=ci):
                        if ci >= 2:
                            S.w(eng, "sc", M[f"sc_wm{ci-2}"])
                        S.inc("ld", eng.dma_start(
                            out=wmf[ci % 2][:],
                            in_=WMODS.rearrange("(k p) c -> p k c", p=128)[:, :, 256 * ci:256 * (ci + 1)]), 16)
                        M[f"ld_wm{ci}"] = S.n["ld"]

                    @blk.scalar
                    def _(eng, ci=ci):
                        S.w(eng, "ld", M[f"ld_wm{ci}"])
                        if ci >= 2:
                            S.w(eng, "tn", M[f"tn_mod{2*(ci-1)-1}"])
                        S.inc("sc", eng.copy(out=wmb[ci % 2][:], in_=wmf[ci % 2][:]))
                        M[f"sc_wm{ci}"] = S.n["sc"]

                    for mp in range(1):
                        @blk.tensor
                        def _(eng, ci=ci, mp=mp):
                            m0 = 2 * ci + 2 * mp
                            if ci == 0 and mp == 0:
                                S.w(eng, "sc", M["sc_silu"])
                            S.w(eng, "sc", M[f"sc_wm{ci}"])
                            for mi in range(2):
                                m = m0 + mi
                                ml = 2 * mp + mi
                                ps = BK.use(eng, S, 6 + mi)
                                for kt in range(2):
                                    ins = eng.matmul(ps[:, 0:2], wmb[ci % 2][:, kt, 128 * ml:128 * (ml + 1)],
                                                     ctsilu[:, kt, :], start=(kt == 0), stop=(kt == 1))
                                S.inc("tn", ins)
                                M[f"tn_mod{m}"] = S.n["tn"]

                        @blk.vector
                        def _(eng, ci=ci, mp=mp):
                            m0 = 2 * ci + 2 * mp
                            for mi in range(2):
                                m = m0 + mi
                                S.w(eng, "tn", M[f"tn_mod{m}"])
                                S.inc("vt", eng.tensor_copy(out=modT[:, m, :], in_=pbs[6 + mi][:, 0:2]))
                                BK.free_when(6 + mi, "vt", S.n["vt"])
                M["vt_modT"] = S.n["vt"]

                @blk.gpsimd
                def _(eng):
                    S.w(eng, "vt", M["vt_modT"])
                    S.inc("gp", eng.dma_start(out=mod_in.ap(),
                                              in_=modT[:].rearrange("p m b -> p (m b)")), 16)
                    S.w(eng, "gp", S.n["gp"])
                    S.inc("cc", eng.collective_compute(
                        "AllReduce", ALU.add, replica_groups=G8,
                        ins=[mod_in.ap().opt()], outs=[mod_out.ap().opt()]))
                    S.w(eng, "cc", S.n["cc"])
                    S.inc("gp", eng.dma_start(out=modT[:],
                                              in_=mod_out.ap().rearrange("p (m b) -> p m b", b=2)), 16)
                    M["gp_mod"] = S.n["gp"]

                @blk.vector
                def _(eng):
                    S.w(eng, "gp", M["gp_mod"])
                    S.inc("vt", eng.tensor_scalar(modv[:], modT[:, :, 0], cst[:, 1:2], 0.0, ALU.mult, ALU.add))
                    S.w(eng, "vt", S.n["vt"])
                    S.inc("vt", eng.scalar_tensor_tensor(modv[:], modT[:, :, 1], cst[:, 2:3], modv[:], ALU.mult, ALU.add))
                    S.w(eng, "vt", S.n["vt"])
                    S.inc("vt", eng.tensor_scalar(ca1[:], modv[:, 16:32], 1.0, 0.0, ALU.add, ALU.add))
                    S.inc("vt", eng.tensor_scalar(cb1[:], modv[:, 64:80], 1.0, 0.0, ALU.add, ALU.add))
                    M["vt_modsel"] = S.n["vt"]

                @blk.scalar
                def _(eng):
                    S.w(eng, "vt", M["vt_modsel"])
                    S.inc("sc", eng.activation(sgn16[:], modv[:, 32:48], AF.Sign))
                    S.inc("sc", eng.activation(ab16[:], modv[:, 32:48], AF.Abs))
                    M["sc_ga"] = S.n["sc"]

                @blk.vector
                def _(eng):
                    S.w(eng, "sc", M["sc_ga"])
                    S.inc("vt", eng.tensor_scalar(ab16[:], ab16[:], 0.25, 0.0, ALU.max, ALU.add))
                    S.w(eng, "vt", S.n["vt"])
                    S.inc("vt", eng.tensor_mul(ga_f[:], sgn16[:], ab16[:]))
                    M["vt_ga"] = S.n["vt"]

                @blk.scalar
                def _(eng):
                    S.w(eng, "vt", M["vt_ga"])
                    S.inc("sc", eng.activation(sgn16[:], modv[:, 80:96], AF.Sign))
                    S.inc("sc", eng.activation(ab16[:], modv[:, 80:96], AF.Abs))
                    M["sc_gb"] = S.n["sc"]

                @blk.vector
                def _(eng):
                    S.w(eng, "sc", M["sc_gb"])
                    S.inc("vt", eng.tensor_scalar(ab16[:], ab16[:], 0.25, 0.0, ALU.max, ALU.add))
                    S.w(eng, "vt", S.n["vt"])
                    S.inc("vt", eng.tensor_mul(gb_f[:], sgn16[:], ab16[:]))
                    M["vt_gb"] = S.n["vt"]

                for j in range(NJ):
                    @blk.scalar
                    def _(eng, j=j):
                        S.w(eng, "ld", M["ld_xg"][j // 4])
                        if j > 0:
                            S.w(eng, "tn", M[f"tn_st1_{j-1}"])
                        S.inc("sc", eng.copy(out=xbf[:], in_=x_sb[:, j, :]))
                        S.inc("sc", eng.activation(xsq[:], x_sb[:, j, :], AF.Square))
                        M[f"sc_x{j}"] = S.n["sc"]

                    @blk.tensor
                    def _(eng, j=j):
                        S.w(eng, "sc", M[f"sc_x{j}"])
                        for c, (c0, c1) in enumerate(CH1):
                            ps = BK.use(eng, S, c) if j == 0 else pbs[c]
                            eng.matmul(ps[0:1, 0:c1 - c0], ones_bf[:], xbf[:, c0:c1],
                                       start=(j == 0), stop=(j == NJ - 1))
                            ps2 = BK.use(eng, S, 3 + c) if j == 0 else pbs[3 + c]
                            ins = eng.matmul(ps2[0:1, 0:c1 - c0], ones_bf[:], xsq[:, c0:c1],
                                             start=(j == 0), stop=(j == NJ - 1))
                        S.inc("tn", ins)
                        M[f"tn_st1_{j}"] = S.n["tn"]
                M["tn_ln1"] = S.n["tn"]

                @blk.vector
                def _(eng):
                    S.w(eng, "tn", M["tn_ln1"])
                    for c, (c0, c1) in enumerate(CH1):
                        eng.tensor_copy(out=rsrow[0:1, c0:c1], in_=pbs[c][0:1, 0:c1 - c0])
                        eng.tensor_copy(out=mrsrow[0:1, c0:c1], in_=pbs[3 + c][0:1, 0:c1 - c0])
                    eng.tensor_scalar(s_mean[:], rsrow[:], 1.0 / D, 0.0, ALU.mult, ALU.add)
                    S.inc("vt", eng.tensor_scalar(s_tmp[:], mrsrow[:], 1.0 / D, 0.0, ALU.mult, ALU.add))
                    M["vt_st1"] = S.n["vt"]
                for c in range(6):
                    BK.free_when(c, "vt", M["vt_st1"])

                @blk.scalar
                def _(eng):
                    S.w(eng, "vt", M["vt_st1"])
                    S.inc("sc", eng.activation(mrsrow[:], s_mean[:], AF.Square))
                    M["sc_msq1"] = S.n["sc"]

                @blk.vector
                def _(eng):
                    S.w(eng, "sc", M["sc_msq1"])
                    S.inc("vt", eng.tensor_sub(rsrow[:], s_tmp[:], mrsrow[:]))
                    M["vt_var1"] = S.n["vt"]

                @blk.scalar
                def _(eng):
                    S.w(eng, "vt", M["vt_var1"])
                    S.inc("sc", eng.activation(mrsrow[:], rsrow[:], AF.Sqrt, bias=epsln[0:1, 0:1]))
                    M["sc_sqrt1"] = S.n["sc"]

                @blk.vector
                def _(eng):
                    S.w(eng, "sc", M["sc_sqrt1"])
                    eng.reciprocal(rsrow[:], mrsrow[:])
                    eng.tensor_scalar(s_tmp[:], s_mean[:], -1.0, 0.0, ALU.mult, ALU.add)
                    S.inc("vt", eng.tensor_mul(mrsrow[:], s_tmp[:], rsrow[:]))
                    M["vt_rs1"] = S.n["vt"]

                for c, (c0, c1) in enumerate(CH1):
                    @blk.tensor
                    def _(eng, c=c, c0=c0, c1=c1):
                        if c == 0:
                            S.w(eng, "vt", M["vt_rs1"])
                        ps = BK.use(eng, S, 0)
                        eng.matmul(ps[:, 0:c1 - c0], ones_row[:], rsrow[0:1, c0:c1], start=True, stop=True)
                        ps = BK.use(eng, S, 1)
                        ins = eng.matmul(ps[:, 0:c1 - c0], ones_row[:], mrsrow[0:1, c0:c1], start=True, stop=True)
                        S.inc("tn", ins)
                        M[f"tn_bc1_{c}"] = S.n["tn"]

                    @blk.vector
                    def _(eng, c=c, c0=c0, c1=c1):
                        S.w(eng, "tn", M[f"tn_bc1_{c}"])
                        eng.tensor_copy(out=rs_bc[:, c0:c1], in_=pbs[0][:, 0:c1 - c0])
                        S.inc("vt", eng.tensor_copy(out=mrs_bc[:, c0:c1], in_=pbs[1][:, 0:c1 - c0]))
                        BK.free_when(0, "vt", S.n["vt"])
                        BK.free_when(1, "vt", S.n["vt"])
                M["vt_bc1"] = S.n["vt"]

                for j in range(NJ):
                    @blk.vector
                    def _(eng, j=j):
                        if j == 0:
                            S.w(eng, "vt", M["vt_bc1"])
                            S.w(eng, "vt", M["vt_modsel"])
                        S.inc("vt", eng.tensor_mul(tmpW[:], x_sb[:, j, :], rs_bc[:]))
                        S.inc("vt", eng.scalar_tensor_tensor(tmpW[:], mrs_bc[:], one_col[:, 0:1], tmpW[:],
                                                             ALU.mult, ALU.add))
                        S.inc("vt", eng.tensor_scalar(h_bf[:, j, :], tmpW[:], ca1[:, j:j + 1], modv[:, j:j + 1],
                                                      ALU.mult, ALU.add))
                        S.w(eng, "vt", S.n["vt"])
                        S.inc("vt", eng.tensor_scalar(h_bf[:, j, 0:1], h_bf[:, j, 0:1],
                                                      cst[:, 0:1], 0.0, ALU.mult, ALU.add))
                M["vt_hdone"] = S.n["vt"]

            with ExitStack() as B:
                eB = B.enter_context
                a_bf = eB(nc.sbuf_tensor("a_bf", [128, NJ, T], BF16))
                ek_bf = eB(nc.sbuf_tensor("ek_bf", [128, NJ, T], BF16))
                wfull = eB(nc.sbuf_tensor("wfull", [128, T], F32))
                scnd = eB(nc.sbuf_tensor("scnd", [128, T], F32))
                af32 = eB(nc.sbuf_tensor("af32", [128, T], F32))
                ef32 = eB(nc.sbuf_tensor("ef32", [128, T], F32))
                xrel = eB(nc.sbuf_tensor("xrel", [128, T], F32))

                for j in range(NJ):
                    @blk.sync
                    def _(eng, j=j):
                        si, st_ = stF.nxt(eng, S)
                        S.inc("ld", eng.dma_start(
                            out=st_[:], in_=WKT.rearrange("(k p) e -> p k e", p=128)[:, :, 128 * j:128 * (j + 1)]), 16)
                        M[f"ld_wk{j}"] = (si, S.n["ld"])
                        si2, st2 = stF.nxt(eng, S)
                        S.inc("ld", eng.dma_start(
                            out=st2[:], in_=WVT.rearrange("(k p) e -> p k e", p=128)[:, :, 128 * j:128 * (j + 1)]), 16)
                        M[f"ld_wv{j}"] = (si2, S.n["ld"])

                    @blk.scalar
                    def _(eng, j=j):
                        si, lv = M[f"ld_wk{j}"]
                        S.w(eng, "ld", lv)
                        if j > 0:
                            S.w(eng, "tn", M[f"tn_k{j-1}"])
                        S.inc("sc", eng.copy(out=wbfA[:], in_=stf[si][:]))
                        M[f"sc_wk{j}"] = S.n["sc"]
                        stF.free_when(si, "sc", S.n["sc"])
                        si2, lv2 = M[f"ld_wv{j}"]
                        S.w(eng, "ld", lv2)
                        if j > 0:
                            S.w(eng, "tn", M[f"tn_v{j-1}"])
                        S.inc("sc", eng.copy(out=wbfB[:], in_=stf[si2][:]))
                        M[f"sc_wv{j}"] = S.n["sc"]
                        stF.free_when(si2, "sc", S.n["sc"])

                    @blk.tensor
                    def _(eng, j=j):
                        if j == 0:
                            S.w(eng, "vt", M["vt_hdone"])
                        S.w(eng, "sc", M[f"sc_wk{j}"])
                        bK = (j % 2) * 4
                        for c, (c0, c1) in enumerate(CH):
                            ps = BK.use(eng, S, bK + c)
                            for kt in range(NJ):
                                ins = eng.matmul(ps[:], wbfA[:, kt, :], h_bf[:, kt, c0:c1],
                                                 start=(kt == 0), stop=(kt == NJ - 1))
                            S.inc("tn", ins)
                        M[f"tn_k{j}"] = S.n["tn"]
                        S.w(eng, "sc", M[f"sc_wv{j}"])
                        for c, (c0, c1) in enumerate(CH):
                            ps = BK.use(eng, S, bK + 2 + c)
                            for kt in range(NJ):
                                ins = eng.matmul(ps[:], wbfB[:, kt, :], h_bf[:, kt, c0:c1],
                                                 start=(kt == 0), stop=(kt == NJ - 1))
                            S.inc("tn", ins)
                        M[f"tn_v{j}"] = S.n["tn"]

                    @blk.scalar
                    def _(eng, j=j):
                        S.w(eng, "tn", M[f"tn_k{j}"])
                        if j > 0:
                            S.w(eng, "vt", M[f"vt_fin{j-1}"])
                        bK = (j % 2) * 4
                        for c, (c0, c1) in enumerate(CH):
                            S.inc("sc", eng.activation(ef32[:, c0:c1], pbs[bK + c][:], AF.Exp))
                        S.w(eng, "sc", S.n["sc"])
                        S.inc("sc", eng.copy(out=ek_bf[:, j, :], in_=ef32[:]))
                        M[f"sc_ek{j}"] = S.n["sc"]
                        BK.free_when(bK + 0, "sc", S.n["sc"])
                        BK.free_when(bK + 1, "sc", S.n["sc"])

                    @blk.vector
                    def _(eng, j=j):
                        S.w(eng, "tn", M[f"tn_v{j}"])
                        S.w(eng, "sc", M[f"sc_ek{j}"])
                        bK = (j % 2) * 4
                        for c, (c0, c1) in enumerate(CH):
                            S.inc("vt", eng.scalar_tensor_tensor(af32[:, c0:c1], pbs[bK + 2 + c][:], one_col[:, 0:1],
                                                                 ef32[:, c0:c1], ALU.mult, ALU.mult))
                        S.w(eng, "vt", S.n["vt"])
                        S.inc("vt", eng.tensor_copy(out=a_bf[:, j, :], in_=af32[:]))
                        BK.free_when(bK + 2, "vt", S.n["vt"])
                        BK.free_when(bK + 3, "vt", S.n["vt"])
                        eng.memset(wfull[:], 0.0)
                        S.inc("vt", eng.tensor_scalar(wfull[:], wfull[:], 0.0, wvec[:, j:j + 1], ALU.mult, ALU.add))
                        S.w(eng, "vt", S.n["vt"])
                        S.inc("vt", eng.tensor_tensor_scan(scnd[:], wfull[:], af32[:], 0.0, ALU.mult, ALU.add))
                        S.w(eng, "vt", S.n["vt"])
                        S.inc("vt", eng.tensor_copy(out=fins[:, 2 * j:2 * j + 1], in_=scnd[:, T - 1:T]))
                        S.inc("vt", eng.tensor_tensor_scan(scnd[:], wfull[:], ef32[:], 0.0, ALU.mult, ALU.add))
                        S.w(eng, "vt", S.n["vt"])
                        S.inc("vt", eng.tensor_copy(out=fins[:, 2 * j + 1:2 * j + 2], in_=scnd[:, T - 1:T]))
                        M[f"vt_fin{j}"] = S.n["vt"]
                M["vt_finall"] = S.n["vt"]

                su_srcs = []
                for ci in range(8):
                    su_srcs.append(SUU.rearrange("(k p) c -> p k c", p=128)[:, :, 128 * ci:128 * (ci + 1)])
                for ci in range(8):
                    su_srcs.append(SUG.rearrange("(k p) c -> p k c", p=128)[:, :, 128 * ci:128 * (ci + 1)])
                for ci in range(8):
                    su_srcs.append(SUD.rearrange("(k p) c -> p k c", p=128)
                                   [:, 16 * (ci % 4):16 * (ci % 4) + 16, 128 * (ci // 4):128 * (ci // 4) + 128])

                for ch, src in enumerate(su_srcs):
                    @blk.sync
                    def _(eng, ch=ch, src=src):
                        si, st_ = stF.nxt(eng, S)
                        S.inc("ld", eng.dma_start(out=st_[:], in_=src), 16)
                        M[f"ld_su{ch}"] = (si, S.n["ld"])

                    @blk.vector
                    def _(eng, ch=ch):
                        si, lv = M[f"ld_su{ch}"]
                        S.w(eng, "ld", lv)
                        S.inc("vt", eng.tensor_reduce(suacc[:, ch:ch + 1], stf[si][:].rearrange("p k c -> p (k c)"), AX.X, ALU.add,
                                                      apply_absolute_value=True))
                        stF.free_when(si, "vt", S.n["vt"])
                M["vt_su"] = S.n["vt"]

                @blk.tensor
                def _(eng):
                    S.w(eng, "vt", M["vt_su"])
                    ps = BK.use(eng, S, 6)
                    eng.matmul(ps[0:1, 0:8], one_col[:, 0:1], suacc[:, 0:8], start=True, stop=True)
                    eng.matmul(ps[0:1, 8:16], one_col[:, 0:1], suacc[:, 8:16], start=True, stop=True)
                    S.inc("tn", eng.matmul(ps[0:1, 16:24], one_col[:, 0:1], suacc[:, 16:24],
                                           start=True, stop=True))
                    M["tn_su"] = S.n["tn"]

                @blk.vector
                def _(eng):
                    S.w(eng, "tn", M["tn_su"])
                    eng.tensor_reduce(su_sb[0:1, 0:1], pbs[6][0:1, 0:8], AX.X, ALU.add)
                    eng.tensor_reduce(su_sb[0:1, 1:2], pbs[6][0:1, 8:16], AX.X, ALU.add)
                    S.inc("vt", eng.tensor_reduce(su_sb[0:1, 2:3], pbs[6][0:1, 16:24], AX.X, ALU.add))
                    M["vt_su3"] = S.n["vt"]
                    BK.free_when(6, "vt", S.n["vt"])

                @blk.gpsimd
                def _(eng):
                    S.w(eng, "vt", M["vt_su3"])
                    S.inc("gp", eng.dma_start(out=su_in.ap(), in_=su_sb[:]), 16)
                    S.w(eng, "gp", S.n["gp"])
                    S.inc("cc", eng.collective_compute(
                        "AllReduce", ALU.add, replica_groups=G8,
                        ins=[su_in.ap().opt()], outs=[su_out.ap().opt()]))
                    S.w(eng, "cc", S.n["cc"])
                    S.inc("gp", eng.dma_start(out=su1[:], in_=su_out.ap()), 16)
                    M["gp_su"] = S.n["gp"]

                @blk.vector
                def _(eng):
                    S.w(eng, "gp", M["gp_su"])
                    S.inc("vt", eng.tensor_scalar(su1[:], su1[:], 1.0 / NELEM, 1e-8, ALU.mult, ALU.add))
                    M["vt_su1"] = S.n["vt"]

                @blk.tensor
                def _(eng):
                    S.w(eng, "vt", M["vt_su1"])
                    ps = BK.use(eng, S, 6)
                    S.inc("tn", eng.matmul(ps[:, 0:3], ones_row[:], su1[0:1, 0:3], start=True, stop=True))
                    M["tn_subc"] = S.n["tn"]

                @blk.vector
                def _(eng):
                    S.w(eng, "tn", M["tn_subc"])
                    S.inc("vt", eng.tensor_copy(out=su_bc[:], in_=pbs[6][:, 0:3]))
                    S.w(eng, "vt", S.n["vt"])
                    S.inc("vt", eng.tensor_scalar(su2[:], su_bc[:], 0.5, 0.0, ALU.mult, ALU.add))
                    S.inc("vt", eng.tensor_scalar(nsu2[:], su_bc[:], -0.5, 0.0, ALU.mult, ALU.add))
                    S.w(eng, "vt", S.n["vt"])
                    S.inc("vt", eng.tensor_scalar(gbsd[:], gb_f[:], su2[:, 2:3], 0.0, ALU.mult, ALU.add))
                    M["vt_su2"] = S.n["vt"]
                    BK.free_when(6, "vt", S.n["vt"])

                @blk.gpsimd
                def _(eng):
                    S.w(eng, "vt", M["vt_finall"])
                    S.inc("gp", eng.dma_start(out=fin_in.ap(), in_=fins[:]), 16)
                    S.w(eng, "gp", S.n["gp"])
                    S.inc("cc", eng.collective_compute(
                        "AllGather", ALU.bypass, replica_groups=G8,
                        ins=[fin_in.ap().opt()], outs=[fin_out.ap().opt()]))
                    S.w(eng, "cc", S.n["cc"])
                    S.inc("gp", eng.dma_start(out=fing[:],
                                              in_=fin_out.ap().rearrange("(g p) c -> p g c", p=128)), 16)
                    M["gp_fin"] = S.n["gp"]

                @blk.vector
                def _(eng):
                    S.w(eng, "gp", M["gp_fin"])
                    S.w(eng, "vt", M["vt_coef"])
                    S.inc("vt", eng.tensor_mul(prodb[:], fing[:], coefful[:].rearrange("p j t b -> p j (t b)")))
                    S.w(eng, "vt", S.n["vt"])
                    S.inc("vt", eng.tensor_add(red4[:], prodb[:, 0:4, :], prodb[:, 4:8, :]))
                    S.w(eng, "vt", S.n["vt"])
                    S.inc("vt", eng.tensor_add(red2[:], red4[:, 0:2, :], red4[:, 2:4, :]))
                    S.w(eng, "vt", S.n["vt"])
                    S.inc("vt", eng.tensor_add(carry[:], red2[:, 0, :], red2[:, 1, :]))
                    M["vt_carry"] = S.n["vt"]

                for j in range(NJ):
                    @blk.sync
                    def _(eng, j=j):
                        si, st_ = stF.nxt(eng, S)
                        S.inc("ld", eng.dma_start(
                            out=st_[:], in_=WRT.rearrange("(k p) e -> p k e", p=128)[:, :, 128 * j:128 * (j + 1)]), 16)
                        M[f"ld_wr{j}"] = (si, S.n["ld"])

                    @blk.scalar
                    def _(eng, j=j):
                        si, lv = M[f"ld_wr{j}"]
                        S.w(eng, "ld", lv)
                        if j > 0:
                            S.w(eng, "tn", M[f"tn_r{j-1}"])
                        S.inc("sc", eng.copy(out=wbfA[:], in_=stf[si][:]))
                        M[f"sc_wr{j}"] = S.n["sc"]
                        stF.free_when(si, "sc", S.n["sc"])

                    @blk.tensor
                    def _(eng, j=j):
                        S.w(eng, "sc", M[f"sc_wr{j}"])
                        bR = (j % 2) * 2
                        for c, (c0, c1) in enumerate(CH):
                            ps = BK.use(eng, S, bR + c)
                            for kt in range(NJ):
                                ins = eng.matmul(ps[:], wbfA[:, kt, :], h_bf[:, kt, c0 + 1:c1 + 1],
                                                 start=(kt == 0), stop=(kt == NJ - 1))
                            S.inc("tn", ins)
                        M[f"tn_r{j}"] = S.n["tn"]

                    @blk.gpsimd
                    def _(eng, j=j):
                        if j > 0:
                            S.w(eng, "vt", M[f"vt_x2{j-1}"])
                            S.w(eng, "sc", M[f"sc_x2{j-1}"])
                        S.inc("gp", eng.dma_start(
                            out=xrel[:], in_=XT.rearrange("(jj p) t -> p jj t", p=128)[:, j, 1:TW]), 16)
                        M[f"gp_xr{j}"] = S.n["gp"]

                    @blk.vector
                    def _(eng, j=j):
                        if j == 0:
                            S.w(eng, "vt", M["vt_carry"])
                        eng.tensor_copy(out=af32[:], in_=a_bf[:, j, :])
                        S.inc("vt", eng.scalar_tensor_tensor(af32[:, 0:1], carry[:, 2 * j:2 * j + 1], wvec[:, j:j + 1],
                                                             af32[:, 0:1], ALU.mult, ALU.add))
                        S.w(eng, "vt", S.n["vt"])
                        eng.memset(wfull[:], 0.0)
                        S.inc("vt", eng.tensor_scalar(wfull[:], wfull[:], 0.0, wvec[:, j:j + 1], ALU.mult, ALU.add))
                        S.w(eng, "vt", S.n["vt"])
                        S.inc("vt", eng.tensor_tensor_scan(scnd[:], wfull[:], af32[:], 0.0, ALU.mult, ALU.add))
                        M[f"vt_sn{j}"] = S.n["vt"]
                        eng.tensor_copy(out=ef32[:], in_=ek_bf[:, j, :])
                        S.inc("vt", eng.scalar_tensor_tensor(ef32[:, 0:1], carry[:, 2 * j + 1:2 * j + 2], wvec[:, j:j + 1],
                                                             ef32[:, 0:1], ALU.mult, ALU.add))
                        S.w(eng, "vt", S.n["vt"])
                        S.inc("vt", eng.tensor_tensor_scan(af32[:], wfull[:], ef32[:], 0.0, ALU.mult, ALU.add))
                        S.w(eng, "vt", S.n["vt"])
                        S.inc("vt", eng.tensor_scalar(af32[:], af32[:], 1e-8, 0.0, ALU.add, ALU.add))
                        S.w(eng, "vt", S.n["vt"])
                        S.inc("vt", eng.reciprocal(ef32[:], af32[:]))
                        S.w(eng, "vt", S.n["vt"])
                        S.inc("vt", eng.scalar_tensor_tensor(scnd[:], ef32[:], one_col[:, 0:1], scnd[:],
                                                             ALU.mult, ALU.mult))
                        S.w(eng, "vt", S.n["vt"])
                        S.w(eng, "tn", M[f"tn_r{j}"])
                        bR = (j % 2) * 2
                        for c, (c0, c1) in enumerate(CH):
                            ins = eng.scalar_tensor_tensor(ef32[:, c0:c1], pbs[bR + c][:], one_col[:, 0:1],
                                                           scnd[:, c0:c1], ALU.mult, ALU.mult)
                        S.inc("vt", ins)
                        S.w(eng, "vt", S.n["vt"])
                        BK.free_when(bR + 0, "vt", S.n["vt"])
                        BK.free_when(bR + 1, "vt", S.n["vt"])
                        S.w(eng, "gp", M[f"gp_xr{j}"])
                        S.inc("vt", eng.scalar_tensor_tensor(xrel[:], ef32[:], ga_f[:, j:j + 1], xrel[:],
                                                             ALU.mult, ALU.add))
                        M[f"vt_x2{j}"] = S.n["vt"]

                    @blk.scalar
                    def _(eng, j=j):
                        S.w(eng, "vt", M[f"vt_x2{j}"])
                        if j > 0:
                            S.w(eng, "tn", M[f"tn_st2_{j-1}"])
                        S.inc("sc", eng.copy(out=xbf[:, 0:T], in_=xrel[:]))
                        S.inc("sc", eng.activation(xsq[:, 0:T], xrel[:], AF.Square))
                        M[f"sc_x2{j}"] = S.n["sc"]

                    @blk.gpsimd
                    def _(eng, j=j):
                        S.w(eng, "vt", M[f"vt_x2{j}"])
                        S.inc("gp", eng.dma_start(out=park.ap()[:, j, :], in_=xrel[:]), 16)
                        M[f"gp_park{j}"] = S.n["gp"]

                    @blk.tensor
                    def _(eng, j=j):
                        S.w(eng, "sc", M[f"sc_x2{j}"])
                        for c, (c0, c1) in enumerate(CH):
                            ps = BK.use(eng, S, 4 + c) if j == 0 else pbs[4 + c]
                            eng.matmul(ps[0:1, :], ones_bf[:], xbf[:, c0:c1],
                                       start=(j == 0), stop=(j == NJ - 1))
                            ps2 = BK.use(eng, S, 6 + c) if j == 0 else pbs[6 + c]
                            ins = eng.matmul(ps2[0:1, :], ones_bf[:], xsq[:, c0:c1],
                                             start=(j == 0), stop=(j == NJ - 1))
                        S.inc("tn", ins)
                        M[f"tn_st2_{j}"] = S.n["tn"]
                M["tn_ln2"] = S.n["tn"]

                @blk.vector
                def _(eng):
                    S.w(eng, "tn", M["tn_ln2"])
                    for c, (c0, c1) in enumerate(CH):
                        eng.tensor_copy(out=rsrow[0:1, c0:c1], in_=pbs[4 + c][0:1, :])
                        eng.tensor_copy(out=mrsrow[0:1, c0:c1], in_=pbs[6 + c][0:1, :])
                    eng.tensor_scalar(s_mean[0:1, 0:T], rsrow[0:1, 0:T], 1.0 / D, 0.0, ALU.mult, ALU.add)
                    S.inc("vt", eng.tensor_scalar(s_tmp[0:1, 0:T], mrsrow[0:1, 0:T], 1.0 / D, 0.0, ALU.mult, ALU.add))
                    M["vt_st2"] = S.n["vt"]
                for c in range(4, 8):
                    BK.free_when(c, "vt", M["vt_st2"])

                @blk.scalar
                def _(eng):
                    S.w(eng, "vt", M["vt_st2"])
                    S.inc("sc", eng.activation(mrsrow[0:1, 0:T], s_mean[0:1, 0:T], AF.Square))
                    M["sc_msq2"] = S.n["sc"]

                @blk.vector
                def _(eng):
                    S.w(eng, "sc", M["sc_msq2"])
                    S.inc("vt", eng.tensor_sub(rsrow[0:1, 0:T], s_tmp[0:1, 0:T], mrsrow[0:1, 0:T]))
                    M["vt_var2"] = S.n["vt"]

                @blk.scalar
                def _(eng):
                    S.w(eng, "vt", M["vt_var2"])
                    S.inc("sc", eng.activation(mrsrow[0:1, 0:T], rsrow[0:1, 0:T], AF.Sqrt, bias=epsln[0:1, 0:1]))
                    M["sc_sqrt2"] = S.n["sc"]

                @blk.vector
                def _(eng):
                    S.w(eng, "sc", M["sc_sqrt2"])
                    eng.reciprocal(s_tmp[0:1, 0:T], mrsrow[0:1, 0:T])
                    eng.tensor_copy(out=rsrow[0:1, 0:T], in_=s_tmp[0:1, 0:T])
                    eng.tensor_scalar(s_tmp[0:1, 0:T], s_mean[0:1, 0:T], -1.0, 0.0, ALU.mult, ALU.add)
                    S.inc("vt", eng.tensor_mul(mrsrow[0:1, 0:T], s_tmp[0:1, 0:T], rsrow[0:1, 0:T]))
                    M["vt_rs2"] = S.n["vt"]

                for c, (c0, c1) in enumerate(CH):
                    @blk.tensor
                    def _(eng, c=c, c0=c0, c1=c1):
                        if c == 0:
                            S.w(eng, "vt", M["vt_rs2"])
                        ps = BK.use(eng, S, 0)
                        eng.matmul(ps[:], ones_row[:], rsrow[0:1, c0:c1], start=True, stop=True)
                        ps = BK.use(eng, S, 1)
                        ins = eng.matmul(ps[:], ones_row[:], mrsrow[0:1, c0:c1], start=True, stop=True)
                        S.inc("tn", ins)
                        M[f"tn_bcx{c}"] = S.n["tn"]

                    @blk.vector
                    def _(eng, c=c, c0=c0, c1=c1):
                        S.w(eng, "tn", M[f"tn_bcx{c}"])
                        eng.tensor_copy(out=rs_bc[:, c0:c1], in_=pbs[0][:])
                        S.inc("vt", eng.tensor_copy(out=mrs_bc[:, c0:c1], in_=pbs[1][:]))
                        M[f"vt_bcx{c}"] = S.n["vt"]
                        BK.free_when(0, "vt", S.n["vt"])
                        BK.free_when(1, "vt", S.n["vt"])

                for j in range(NJ):
                    @blk.gpsimd
                    def _(eng, j=j):
                        if j == 0:
                            S.w(eng, "vt", M[f"vt_x2{NJ-1}"])
                            S.w(eng, "sc", M[f"sc_x2{NJ-1}"])
                        else:
                            S.w(eng, "vt", M[f"vt_h2{j-1}"])
                        S.inc("gp", eng.dma_start(out=xrel[:], in_=park.ap()[:, j, :]), 16)
                        M[f"gp_h2r{j}"] = S.n["gp"]

                    @blk.vector
                    def _(eng, j=j):
                        S.w(eng, "gp", M[f"gp_h2r{j}"])
                        if j == 0:
                            S.w(eng, "vt", M["vt_bcx1"])
                        S.inc("vt", eng.tensor_mul(af32[:], xrel[:], rs_bc[:, 0:T]))
                        S.w(eng, "vt", S.n["vt"])
                        S.inc("vt", eng.tensor_add(ef32[:], af32[:], mrs_bc[:, 0:T]))
                        S.w(eng, "vt", S.n["vt"])
                        S.inc("vt", eng.tensor_scalar(h2[:, j, :], ef32[:], cb1[:, j:j + 1],
                                                      modv[:, 48 + j:49 + j], ALU.mult, ALU.add))
                        M[f"vt_h2{j}"] = S.n["vt"]
                M["vt_h2done"] = S.n["vt"]

        with ExitStack() as C:
            eC = C.enter_context
            hid = eC(nc.sbuf_tensor("hid", [128, NE, T], BF16))
            sgB = eC(nc.sbuf_tensor("sgB", [128, NJ, 128], BF16))
            silu_sb = eC(nc.sbuf_tensor("silu_sb", [128, T], F32))
            xrel2 = eC(nc.sbuf_tensor("xrel2", [128, T], F32))
            outsb = [eC(nc.sbuf_tensor("outsb0", [128, T], F32))]

            for et in range(NE):
                @blk.sync
                def _(eng, et=et):
                    si, st_ = stF.nxt(eng, S)
                    S.inc("ld", eng.dma_start(
                        out=st_[:], in_=WUT.rearrange("(k p) e -> p k e", p=128)[:, :, 128 * et:128 * (et + 1)]), 16)
                    M[f"ld_wu{et}"] = (si, S.n["ld"])
                    si2, st2 = stF.nxt(eng, S)
                    S.inc("ld", eng.dma_start(
                        out=st2[:], in_=WGT.rearrange("(k p) e -> p k e", p=128)[:, :, 128 * et:128 * (et + 1)]), 16)
                    M[f"ld_wg{et}"] = (si2, S.n["ld"])

                @blk.scalar
                def _(eng, et=et):
                    si, lv = M[f"ld_wu{et}"]
                    S.w(eng, "ld", lv)
                    if et == 0:
                        S.w(eng, "vt", M["vt_su2"])
                        S.w(eng, "tn", M[f"tn_r{NJ-1}"])
                    else:
                        S.w(eng, "tn", M[f"tn_up{et-1}"])
                        S.w(eng, "vt", M[f"vt_tg{et-1}"])
                    S.inc("sc", eng.activation(wbfA[:], stf[si][:], AF.Sign, bias=nsu2[:, 0:1]))
                    S.inc("sc", eng.activation(sgB[:], stf[si][:], AF.Sign, bias=su2[:, 0:1]))
                    M[f"sc_sgu{et}"] = S.n["sc"]
                    stF.free_when(si, "sc", S.n["sc"])

                @blk.vector
                def _(eng, et=et):
                    S.w(eng, "sc", M[f"sc_sgu{et}"])
                    S.inc("vt", eng.scalar_tensor_tensor(wbfA[:], sgB[:], one_col[:, 0:1], wbfA[:],
                                                         ALU.mult, ALU.add))
                    M[f"vt_tu{et}"] = S.n["vt"]

                @blk.scalar
                def _(eng, et=et):
                    si2, lv2 = M[f"ld_wg{et}"]
                    S.w(eng, "ld", lv2)
                    S.w(eng, "vt", M[f"vt_tu{et}"])
                    if et > 0:
                        S.w(eng, "tn", M[f"tn_gt{et-1}"])
                    S.inc("sc", eng.activation(wbfB[:], stf[si2][:], AF.Sign, bias=nsu2[:, 1:2]))
                    S.inc("sc", eng.activation(sgB[:], stf[si2][:], AF.Sign, bias=su2[:, 1:2]))
                    M[f"sc_sgg{et}"] = S.n["sc"]
                    stF.free_when(si2, "sc", S.n["sc"])

                @blk.vector
                def _(eng, et=et):
                    S.w(eng, "sc", M[f"sc_sgg{et}"])
                    S.inc("vt", eng.scalar_tensor_tensor(wbfB[:], sgB[:], one_col[:, 0:1], wbfB[:],
                                                         ALU.mult, ALU.add))
                    M[f"vt_tg{et}"] = S.n["vt"]

                @blk.tensor
                def _(eng, et=et):
                    if et == 0:
                        S.w(eng, "vt", M["vt_h2done"])
                    S.w(eng, "vt", M[f"vt_tu{et}"])
                    bU = (et % 2) * 4
                    for c, (c0, c1) in enumerate(CH):
                        ps = BK.use(eng, S, bU + c)
                        for kt in range(NJ):
                            ins = eng.matmul(ps[:], wbfA[:, kt, :], h2[:, kt, c0:c1],
                                             start=(kt == 0), stop=(kt == NJ - 1))
                        S.inc("tn", ins)
                    M[f"tn_up{et}"] = S.n["tn"]
                    S.w(eng, "vt", M[f"vt_tg{et}"])
                    for c, (c0, c1) in enumerate(CH):
                        ps = BK.use(eng, S, bU + 2 + c)
                        for kt in range(NJ):
                            ins = eng.matmul(ps[:], wbfB[:, kt, :], h2[:, kt, c0:c1],
                                             start=(kt == 0), stop=(kt == NJ - 1))
                        S.inc("tn", ins)
                    M[f"tn_gt{et}"] = S.n["tn"]

                @blk.scalar
                def _(eng, et=et):
                    S.w(eng, "tn", M[f"tn_up{et}"])
                    bU = (et % 2) * 4
                    for c, (c0, c1) in enumerate(CH):
                        S.inc("sc", eng.activation(silu_sb[:, c0:c1], pbs[bU + c][:], AF.Silu, scale=su2[:, 0:1]))
                    M[f"sc_silu{et}"] = S.n["sc"]
                    BK.free_when(bU + 0, "sc", S.n["sc"])
                    BK.free_when(bU + 1, "sc", S.n["sc"])

                @blk.vector
                def _(eng, et=et):
                    S.w(eng, "sc", M[f"sc_silu{et}"])
                    S.w(eng, "tn", M[f"tn_gt{et}"])
                    bU = (et % 2) * 4
                    for c, (c0, c1) in enumerate(CH):
                        S.inc("vt", eng.scalar_tensor_tensor(hid[:, et, c0:c1], pbs[bU + 2 + c][:], su2[:, 1:2],
                                                             silu_sb[:, c0:c1], ALU.mult, ALU.mult))
                    M[f"vt_hid{et}"] = S.n["vt"]
                    BK.free_when(bU + 2, "vt", S.n["vt"])
                    BK.free_when(bU + 3, "vt", S.n["vt"])
            M["vt_hiddone"] = S.n["vt"]

            @blk.gpsimd
            def _(eng):
                S.w(eng, "vt", M["vt_hiddone"])
                S.inc("gp", eng.dma_start(out=hidspill.ap(), in_=hid[:]), 16)

            for dd in range(NJ):
                for c4 in range(4):
                    gidx = dd * 4 + c4

                    @blk.sync
                    def _(eng, dd=dd, c4=c4):
                        si, st_ = stF.nxt(eng, S)
                        S.inc("ld", eng.dma_start(
                            out=st_[:],
                            in_=WDT.rearrange("(k p) d -> p k d", p=128)[:, 16 * c4:16 * c4 + 16,
                                                                         128 * dd:128 * (dd + 1)]), 16)
                        M[f"ld_wd{gidx}"] = (si, S.n["ld"])

                    @blk.scalar
                    def _(eng, dd=dd, c4=c4, gidx=gidx):
                        si, lv = M[f"ld_wd{gidx}"]
                        S.w(eng, "ld", lv)
                        if gidx >= 2:
                            pd, pc = divmod(gidx - 2, 4)
                            S.w(eng, "tn", M[f"tn_dn{pd}_{pc}"])
                        if gidx >= 1:
                            S.w(eng, "vt", M[f"vt_td{gidx-1}"])
                        elif True:
                            S.w(eng, "vt", M["vt_hiddone"])
                        wslot = wbfA if (gidx % 2 == 0) else wbfB
                        S.inc("sc", eng.activation(wslot[:], stf[si][:], AF.Sign, bias=nsu2[:, 2:3]))
                        S.inc("sc", eng.activation(sgB[:], stf[si][:], AF.Sign, bias=su2[:, 2:3]))
                        M[f"sc_sgd{gidx}"] = S.n["sc"]
                        stF.free_when(si, "sc", S.n["sc"])

                    @blk.vector
                    def _(eng, gidx=gidx):
                        S.w(eng, "sc", M[f"sc_sgd{gidx}"])
                        wslot = wbfA if (gidx % 2 == 0) else wbfB
                        S.inc("vt", eng.scalar_tensor_tensor(wslot[:], sgB[:], one_col[:, 0:1], wslot[:],
                                                             ALU.mult, ALU.add))
                        M[f"vt_td{gidx}"] = S.n["vt"]

                    @blk.tensor
                    def _(eng, dd=dd, c4=c4, gidx=gidx):
                        S.w(eng, "vt", M[f"vt_td{gidx}"])
                        wslot = wbfA if (gidx % 2 == 0) else wbfB
                        bD = (dd % 2) * 2
                        for c, (c0, c1) in enumerate(CH):
                            ps = BK.use(eng, S, bD + c) if c4 == 0 else pbs[bD + c]
                            for kt in range(NJ):
                                et = 16 * c4 + kt
                                ins = eng.matmul(ps[:], wslot[:, kt, :], hid[:, et, c0:c1],
                                                 start=(c4 == 0 and kt == 0), stop=(c4 == 3 and kt == NJ - 1))
                            S.inc("tn", ins)
                        M[f"tn_dn{dd}_{c4}"] = S.n["tn"]

                @blk.gpsimd
                def _(eng, dd=dd):
                    if dd > 0:
                        S.w(eng, "vt", M[f"vt_out{dd-1}"])
                    S.inc("gp", eng.dma_start(out=xrel2[:], in_=park.ap()[:, dd, :]), 16)
                    M[f"gp_x2r{dd}"] = S.n["gp"]

                @blk.vector
                def _(eng, dd=dd):
                    S.w(eng, "tn", M[f"tn_dn{dd}_3"])
                    S.w(eng, "gp", M[f"gp_x2r{dd}"])
                    if dd >= 1:
                        S.w(eng, "gp", M[f"gp_out{dd-1}"])
                    osl = outsb[0]
                    bD = (dd % 2) * 2
                    for c, (c0, c1) in enumerate(CH):
                        S.inc("vt", eng.scalar_tensor_tensor(osl[:, c0:c1], pbs[bD + c][:], gbsd[:, dd:dd + 1],
                                                             xrel2[:, c0:c1], ALU.mult, ALU.add))
                    M[f"vt_out{dd}"] = S.n["vt"]
                    BK.free_when(bD + 0, "vt", S.n["vt"])
                    BK.free_when(bD + 1, "vt", S.n["vt"])

                @blk.gpsimd
                def _(eng, dd=dd):
                    S.w(eng, "vt", M[f"vt_out{dd}"])
                    S.inc("gp", eng.dma_start(out=OUT[:, dd, :], in_=outsb[0][:]), 16)
                    M[f"gp_out{dd}"] = S.n["gp"]

            @blk.gpsimd
            def _(eng):
                S.w(eng, "gp", S.n["gp"])

    return nc


def kernel(L0_attn_x, L0_attn_Wk, L0_attn_Wv, L0_attn_Wr, L0_attn_time_decay,
           L0_ffn_W_up, L0_ffn_W_gate, L0_ffn_W_down, L0_c, L0_wmod):
    x = np.ascontiguousarray(np.asarray(L0_attn_x, dtype=np.float32))
    WKT = np.ascontiguousarray(np.asarray(L0_attn_Wk, np.float32).T)
    WVT = np.ascontiguousarray(np.asarray(L0_attn_Wv, np.float32).T)
    WRT = np.ascontiguousarray(np.asarray(L0_attn_Wr, np.float32).T)
    WUT = np.ascontiguousarray(np.asarray(L0_ffn_W_up, np.float32).T)
    WGT = np.ascontiguousarray(np.asarray(L0_ffn_W_gate, np.float32).T)
    WDT = np.ascontiguousarray(np.asarray(L0_ffn_W_down, np.float32).T)
    tdv = np.asarray(L0_attn_time_decay, np.float32)
    TD = np.ascontiguousarray(tdv.reshape(NJ, 128).T)
    c = np.asarray(L0_c, np.float32)
    wmod = np.asarray(L0_wmod, np.float32)
    cT = np.ascontiguousarray(c.T)

    in_maps = []
    for r in range(8):
        b, q = r // 4, r % 4
        lo = 1024 * q
        if q > 0:
            seg = x[b, lo - 1:lo + 1024]
        else:
            seg = np.concatenate([x[b, 0:1], x[b, 0:1024]], axis=0)
        XTr = np.ascontiguousarray(seg.T)
        WMODSr = np.ascontiguousarray(wmod[256 * r:256 * (r + 1), :])
        CTSr = np.ascontiguousarray(cT[256 * r:256 * (r + 1)].reshape(2, 128, 2).transpose(1, 0, 2))
        consts = np.zeros((128, 20), np.float32)
        consts[:, 0] = 0.0 if q == 0 else 1.0
        consts[:, 1] = 1.0 if b == 0 else 0.0
        consts[:, 2] = 1.0 if b == 1 else 0.0
        for j in range(8):
            jb, jq = j // 4, j % 4
            same = (jb == b) and (jq < q)
            consts[:, 3 + j] = 1024.0 * (q - 1 - jq) if same else 0.0
            consts[:, 11 + j] = 1.0 if same else 0.0
        SUUr = np.ascontiguousarray(WUT[:, 1024 * r:1024 * (r + 1)])
        SUGr = np.ascontiguousarray(WGT[:, 1024 * r:1024 * (r + 1)])
        SUDr = np.ascontiguousarray(WDT[:, 256 * r:256 * (r + 1)])
        in_maps.append(dict(XT=XTr, WKT=WKT, WVT=WVT, WRT=WRT, WUT=WUT, WGT=WGT, WDT=WDT,
                            WMODS=WMODSr, CTS=CTSr, TDI=TD, CONSTS=consts,
                            SUU=SUUr, SUG=SUGr, SUD=SUDr))

    nc = build()
    res = run_bass_kernel_spmd(nc, in_maps, core_ids=list(range(8)))
    out = np.empty((2, 4096, 2048), np.float32)
    for r in range(8):
        b, q = r // 4, r % 4
        o = np.asarray(res.results[r]["OUT"])
        out[b, 1024 * q:1024 * (q + 1), :] = o.transpose(2, 1, 0).reshape(T, D)
    return out
